# version 01
# baseline (speedup 1.0000x reference)
import numpy as np
import ml_dtypes
from contextlib import ExitStack
import concourse.bass as bass
import concourse.mybir as mybir
from concourse.bass_utils import run_bass_kernel_spmd

F32 = mybir.dt.float32
BF16 = mybir.dt.bfloat16
AF = mybir.ActivationFunctionType
ALU = mybir.AluOpType
AX = mybir.AxisListType

T = 2048
NCH = 8
NT = T // 512
ENGS = ("tensor", "vector", "scalar", "gpsimd", "sync")

TWO_PI = 6.283185307179586
CW_C1 = 6.28125
CW_C2 = float(np.float32(TWO_PI - CW_C1))
CW_C3 = TWO_PI - CW_C1 - float(np.float32(TWO_PI - CW_C1))


class PS:
    def __init__(self, ap):
        self.ap = ap
        self.tok_w = None
        self.tok_r = None


class Ctx:
    def __init__(self, nc, es):
        self.nc = nc
        self.es = es
        self.q = {e: [] for e in ENGS}
        self.sems = {}
        self.cnt = {}
        self.waited = {e: {} for e in ENGS}
        self.last = {e: None for e in ENGS}
        for s in ("pt", "pv", "ps", "pg", "dS", "dG", "dA"):
            self._mksem(s)
        self.prog = {"tensor": "pt", "vector": "pv", "scalar": "ps", "gpsimd": "pg"}
        self.dmasem = {"sync": "dS", "gpsimd": "dG", "scalar": "dA"}
        self._sb = {}
        self.uid = 0
        self.psum = []

    def _mksem(self, name):
        self.sems[name] = self.es.enter_context(self.nc.semaphore(name))
        self.cnt[name] = 0

    def sbuf(self, name, shape, dtype, side="left"):
        self.uid += 1
        cm = self.nc.sbuf_tensor(f"{name}_{self.uid}", list(shape), dtype, side=side)
        h = cm.__enter__()
        self._sb[id(h)] = cm
        return h

    def sbfree(self, *handles):
        for h in handles:
            cm = self._sb.pop(id(h))
            cm.__exit__(None, None, None)

    def mkpsum(self, n_f32=6, n_bf16=2):
        for i in range(n_f32):
            h = self.es.enter_context(self.nc.psum_tensor(f"psb{i}", [128, 512], F32))
            self.psum.append(PS(h))
        for i in range(n_bf16):
            h = self.es.enter_context(self.nc.psum_tensor(f"psbb{i}", [128, 1024], BF16))
            self.psum.append(PS(h))

    def op(self, eng, fn, deps=(), sig=False, amt=1, sem=None):
        wl = []
        for d in deps:
            if d is None:
                continue
            s, v = d
            if v > self.waited[eng].get(s, 0):
                self.waited[eng][s] = v
                wl.append((s, v))
        tok = None
        sname = None
        if sig:
            sname = sem or self.prog[eng]
            self.cnt[sname] += amt
            tok = (sname, self.cnt[sname])
            self.last[eng] = tok
        sems = self.sems

        def run(e, wl=wl, fn=fn, sname=sname, amt=amt):
            for s, v in wl:
                e.wait_ge(sems[s], v)
            r = fn(e) if fn is not None else None
            if sname is not None:
                assert r is not None, "sig op must return last instruction"
                r.then_inc(sems[sname], amt)
        self.q[eng].append(run)
        return tok

    def dma(self, eng, out, in_, deps=(), **kw):
        return self.op(eng, lambda e: e.dma_start(out=out, in_=in_, **kw),
                       deps=deps, sig=True, amt=16, sem=self.dmasem[eng])

    def barrier_all(self):
        toks = [(s, self.cnt[s]) for s in self.cnt if self.cnt[s] > 0]
        for e in ("tensor", "vector", "scalar", "gpsimd", "sync"):
            self.op(e, None, deps=toks)

    def flush(self, block):
        for e in ENGS:
            ops = self.q[e]
            if not ops:
                continue
            getattr(block, e)(lambda eng, ops=ops: [f(eng) for f in ops])
        self.q = {e: [] for e in ENGS}


def mm(ctx, ps, out_ap, lh, rh, start, stop, deps=(), sig=False):
    dd = list(deps)
    if start:
        dd.append(ps.tok_r)
    tok = ctx.op("tensor", lambda e: e.matmul(out_ap, lh, rh, start=start, stop=stop),
                 deps=dd, sig=sig or stop)
    if stop:
        ps.tok_w = tok
    return tok


def mm_group(ctx, ps, out_ap, pairs):
    n = len(pairs)
    tok = None
    for i, (lh, rh, dd) in enumerate(pairs):
        tok = mm(ctx, ps, out_ap, lh, rh, start=(i == 0), stop=(i == n - 1), deps=dd)
    return tok


def transpose128(ctx, ps, out_psum_ap, in_ap, ident_ap, deps=()):
    dd = list(deps) + [ps.tok_r]
    tok = ctx.op("tensor", lambda e: e.transpose(out_psum_ap, in_ap, ident_ap),
                 deps=dd, sig=True)
    ps.tok_w = tok
    return tok


def evac(ctx, eng, ps, fn, deps=()):
    tok = ctx.op(eng, fn, deps=list(deps) + [ps.tok_w], sig=True)
    ps.tok_r = tok
    return tok



def build_consts():
    ident = np.eye(128, dtype=np.float32)
    cols = np.zeros((128, 8), dtype=np.float32)
    cols[:, 0] = np.pi / 2
    cols[:, 1] = 1.0
    cols[:, 2] = 2.0 ** 23
    cols[:, 3] = 1e-5
    cols[:, 4] = 1e-6
    cols[:, 5] = -15.0
    e8 = np.zeros((8, 8 * 128), dtype=np.float32)
    for e in range(8):
        e8[e, e * 128:(e + 1) * 128] = 1.0
    ones_col = np.ones((128, 1), dtype=np.float32)
    ones_row = np.ones((1, 128), dtype=np.float32)
    return {"c_ident": ident, "c_cols": cols, "c_e8": e8,
            "c_onescol": ones_col, "c_onesrow": ones_row}


CONST_SPECS = {
    "c_ident": ([128, 128], F32),
    "c_cols": ([128, 8], F32),
    "c_e8": ([8, 8 * 128], F32),
    "c_onescol": ([128, 1], F32),
    "c_onesrow": ([1, 128], F32),
}


def load_consts(ctx, dram):
    c = {}
    toks = []
    for name, (shape, dt) in CONST_SPECS.items():
        h = ctx.sbuf(name, shape, dt)
        c[name] = h
        toks.append(ctx.dma("sync", h[:], dram[name]))
    for name in ("c_ident", "c_e8", "c_onescol", "c_onesrow"):
        shape, _ = CONST_SPECS[name]
        hb = ctx.sbuf(name + "_bf", shape, BF16)
        c[name + "_bf"] = hb
        ctx.op("vector", lambda e, a=hb, b=c[name]: e.tensor_copy(a[:], b[:]),
               deps=toks, sig=True)
    c["rows"] = ctx.sbuf("rows", [1, 1536], F32)
    c["smalls"] = ctx.sbuf("smalls", [128, 512], F32)
    build_cmask(ctx, c)
    c["tok"] = ctx.last["vector"]
    return c



def ln_to(ctx, c, src, out, *, src_f32=True, affine=None, accumulate=False,
          acc_dst=None, deps=(), n_ch=NCH, cast_buf=None, eps_col=3,
          sq_buf=None, tmp_buf=None):
    inv_d = 1.0 / (128 * n_ch)
    rows = c["rows"]
    sx = rows[0:1, 0:512]
    mu = rows[0:1, 512:1024]
    var = rows[0:1, 1024:1536]

    own_cb = False
    if src_f32:
        if cast_buf is None:
            cast_buf = ctx.sbuf("lncb", [128, n_ch * T], BF16)
            own_cb = True
        cast_toks = []
        for ch in range(n_ch):
            t = ctx.op("scalar", lambda e, ch=ch: e.activation(
                cast_buf[:, ch * T:(ch + 1) * T], src(ch), AF.Identity, scale=1.0),
                deps=deps, sig=True)
            cast_toks.append(t)
        bsrc = lambda ch: cast_buf[:, ch * T:(ch + 1) * T]
    else:
        cast_toks = [None] * n_ch
        bsrc = src

    own_sq = sq_buf is None
    sq = sq_buf if sq_buf is not None else ctx.sbuf("lnsq", [128, 2048], BF16)
    tmp = tmp_buf if tmp_buf is not None else ctx.sbuf("lntmp", [128, 1024], F32)
    out_toks = []
    row_tok = None
    prev_apply = {0: None, 1: None}
    for n in range(NT):
        nsl = slice(n * 512, (n + 1) * 512)
        psx = ctx.psum[4]
        psq = ctx.psum[5]
        sq_mm_toks = []
        for ch in range(n_ch):
            dep = [cast_toks[ch]] if src_f32 else list(deps)
            mm(ctx, psx, psx.ap[0:1, :], c["c_onescol_bf"][:], bsrc(ch)[:, nsl],
               start=(ch == 0), stop=(ch == n_ch - 1), deps=dep + [row_tok])
            buf = sq[:, (ch % 4) * 512:(ch % 4) * 512 + 512]
            war = [sq_mm_toks[ch - 4]] if ch >= 4 else []
            sqt = ctx.op("vector", lambda e, buf=buf, ch=ch, nsl=nsl: e.scalar_tensor_tensor(
                buf, bsrc(ch)[:, nsl], 1.0, bsrc(ch)[:, nsl], ALU.mult, ALU.mult),
                deps=dep + war + ([row_tok] if ch < 4 else []), sig=True)
            t = mm(ctx, psq, psq.ap[0:1, :], c["c_onescol_bf"][:], buf,
                   start=(ch == 0), stop=(ch == n_ch - 1), deps=[sqt], sig=True)
            sq_mm_toks.append(t)
        t1 = evac(ctx, "vector", psx, lambda e: e.tensor_scalar(
            mu, psx.ap[0:1, :], inv_d, None, ALU.mult, ALU.bypass))
        t1b = ctx.op("vector", lambda e: e.tensor_tensor(sx, mu, mu, op=ALU.mult),
                     deps=[t1], sig=True)
        t2 = evac(ctx, "vector", psq, lambda e: e.scalar_tensor_tensor(
            var, psq.ap[0:1, :], inv_d, sx, ALU.mult, ALU.subtract), deps=[t1b])
        t3 = ctx.op("scalar", lambda e: e.activation(var, var, AF.Sqrt,
                    bias=c["c_cols"][0:1, eps_col:eps_col + 1], scale=1.0), deps=[t2], sig=True)
        t4 = ctx.op("vector", lambda e: e.reciprocal(var, var), deps=[t3], sig=True)
        row_tok = t4
        bmu = mm(ctx, psx, psx.ap[:, :], c["c_onesrow"][:], mu, start=True,
                 stop=True, deps=[t4])
        brs = mm(ctx, psq, psq.ap[:, :], c["c_onesrow"][:], var, start=True,
                 stop=True, deps=[t4])
        last_mu_read = None
        last_rs_read = None
        for ch in range(n_ch):
            eng = "vector" if ch % 2 == 0 else "gpsimd"
            eng = "vector"
            tbuf = tmp[:, (ch % 2) * 512:(ch % 2) * 512 + 512]
            dep = [bmu, brs, prev_apply[ch % 2]]
            dep += [cast_toks[ch]] if src_f32 else list(deps)
            a1 = ctx.op(eng, lambda e, ch=ch, tbuf=tbuf, nsl=nsl: e.tensor_tensor(
                tbuf, src(ch)[:, nsl] if src_f32 else bsrc(ch)[:, nsl],
                psx.ap[:, :], op=ALU.subtract), deps=dep, sig=True)
            a2 = ctx.op(eng, lambda e, ch=ch, tbuf=tbuf: e.tensor_tensor(
                tbuf, tbuf, psq.ap[:, :], op=ALU.mult), deps=[a1], sig=True)
            if affine is not None:
                g, b = affine
                a3 = ctx.op(eng, lambda e, ch=ch, tbuf=tbuf, nsl=nsl: e.tensor_scalar(
                    out(ch)[:, nsl], tbuf, g(ch), b(ch), ALU.mult, ALU.add),
                    deps=[a2], sig=True)
            elif accumulate:
                a3 = ctx.op(eng, lambda e, ch=ch, tbuf=tbuf, nsl=nsl: e.tensor_tensor(
                    out(ch)[:, nsl], acc_dst(ch)[:, nsl], tbuf, op=ALU.add),
                    deps=[a2], sig=True)
            else:
                a3 = ctx.op(eng, lambda e, ch=ch, tbuf=tbuf, nsl=nsl: e.tensor_copy(
                    out(ch)[:, nsl], tbuf), deps=[a2], sig=True)
            prev_apply[ch % 2] = a3
            last_mu_read = a1
            last_rs_read = a2
            out_toks.append(a3)
        psx.tok_r = last_mu_read
        psq.tok_r = last_rs_read
    frees = [tmp, sq] if own_sq else []
    if own_cb:
        frees.append(cast_buf)
    ctx.barrier_all()
    if frees:
        ctx.sbfree(*frees)
    return out_toks


def moe_experts(ctx, c, tag, yb, xln, wfm, wfm_toks, wspec, mode):
    wb = ctx.sbuf(tag + "_wb", [128, T], F32)
    xe = ctx.sbuf(tag + "_xe", [128, NCH * 1024], BF16) if mode == "input" else None
    he = ctx.sbuf(tag + "_he", [128, 16 * 1024], BF16)
    slabs = [ctx.sbuf(tag + f"_sl{i}", [128, 1024], BF16) for i in range(4)]
    dslabs = [ctx.sbuf(tag + f"_dsl{i}", [128, 2048], BF16) for i in range(2)]
    ab = ctx.sbuf(tag + "_ab", [128, 1024], BF16)
    slab_used = [None] * 4
    dslab_used = [None] * 2
    wb_readers = []
    xe_used = []
    he_used = []

    for (g3, u3, d3, eidx) in wspec:
        scaled = eidx is not None
        if scaled:
            ps3 = ctx.psum[3]
            wb_toks = []
            old_wb_readers = wb_readers
            wb_readers = []
            for n in range(NT):
                mm(ctx, ps3, ps3.ap[:, :], c["c_e8_bf"][0:8, eidx * 128:(eidx + 1) * 128],
                   wfm[0:8, n * 512:(n + 1) * 512], start=True, stop=True,
                   deps=wfm_toks)
                wb_toks.append(evac(ctx, "vector", ps3,
                    lambda e2, n=n, ps3=ps3: e2.tensor_copy(wb[:, n * 512:(n + 1) * 512],
                                                   ps3.ap[:, :]), deps=old_wb_readers))
        for half in range(2):
            hsl0 = half * 1024
            if scaled and mode == "input":
                new_xe_toks = []
                for ch in range(NCH):
                    for n2 in range(2):
                        eng = "vector" if (ch + n2) % 2 == 0 else "gpsimd"
                        asl = slice(hsl0 + n2 * 512, hsl0 + n2 * 512 + 512)
                        xsl = slice(ch * 1024 + n2 * 512, ch * 1024 + n2 * 512 + 512)
                        t = ctx.op(eng, lambda e2, ch=ch, asl=asl, xsl=xsl: e2.tensor_tensor(
                            xe[:, xsl], xln[:, ch * T + asl.start: ch * T + asl.stop],
                            wb[:, asl], op=ALU.mult),
                            deps=wb_toks + xe_used, sig=True)
                        new_xe_toks.append(t)
                        wb_readers.append(t)
                xe_used = []
                xe_toks = new_xe_toks
                xeg = lambda kc, t2: xe[:, kc * 1024 + t2 * 512: kc * 1024 + t2 * 512 + 512]
            else:
                xe_toks = []
                xeg = lambda kc, t2: xln[:, kc * T + hsl0 + t2 * 512: kc * T + hsl0 + t2 * 512 + 512]
            new_he_toks = []
            for m in range(16):
                pidx = m % 2
                g_ap = g3[:, :, m * 128:(m + 1) * 128]
                u_ap = u3[:, :, m * 128:(m + 1) * 128]
                t_g = ctx.dma("gpsimd", slabs[pidx][:], g_ap, deps=[slab_used[pidx]])
                t_u = ctx.dma("gpsimd", slabs[2 + pidx][:], u_ap, deps=[slab_used[2 + pidx]])
                lastA = lastB = None
                for t2 in range(2):
                    psA = ctx.psum[0]
                    psB = ctx.psum[1]
                    pairsA = [(slabs[pidx][:, kc * 128:(kc + 1) * 128], xeg(kc, t2),
                               [t_g] + xe_toks) for kc in range(NCH)]
                    mm_group(ctx, psA, psA.ap[:, :], pairsA)
                    ta = evac(ctx, "scalar", psA,
                        lambda e2, t2=t2, psA=psA: e2.activation(ab[:, t2 * 512:(t2 + 1) * 512],
                                                        psA.ap[:, :], AF.Silu, scale=1.0))
                    pairsB = [(slabs[2 + pidx][:, kc * 128:(kc + 1) * 128], xeg(kc, t2),
                               [t_u] + xe_toks) for kc in range(NCH)]
                    mm_group(ctx, psB, psB.ap[:, :], pairsB)
                    hsl = slice(m * 1024 + t2 * 512, m * 1024 + t2 * 512 + 512)
                    tb = evac(ctx, "vector", psB,
                        lambda e2, hsl=hsl, t2=t2, psB=psB: e2.tensor_tensor(
                            he[:, hsl], ab[:, t2 * 512:(t2 + 1) * 512], psB.ap[:, :],
                            op=ALU.mult), deps=[ta] + he_used)
                    if scaled and mode == "hidden":
                        asl = slice(hsl0 + t2 * 512, hsl0 + t2 * 512 + 512)
                        tb = ctx.op("gpsimd", lambda e2, hsl=hsl, asl=asl: e2.tensor_tensor(
                            he[:, hsl], he[:, hsl], wb[:, asl], op=ALU.mult),
                            deps=[tb] + wb_toks, sig=True)
                        wb_readers.append(tb)
                    new_he_toks.append(tb)
                    lastA, lastB = ta, tb
                slab_used[pidx] = lastA
                slab_used[2 + pidx] = lastB
            he_used = []
            he_toks = new_he_toks
            for oc in range(NCH):
                didx = oc % 2
                d_ap = d3[:, :, oc * 128:(oc + 1) * 128]
                t_d = ctx.dma("gpsimd", dslabs[didx][:], d_ap, deps=[dslab_used[didx]])
                last = None
                for t2 in range(2):
                    psD = ctx.psum[2]
                    pairs = [(dslabs[didx][:, mc * 128:(mc + 1) * 128],
                              he[:, mc * 1024 + t2 * 512: mc * 1024 + t2 * 512 + 512],
                              [t_d] + he_toks) for mc in range(16)]
                    mm_group(ctx, psD, psD.ap[:, :], pairs)
                    asl = slice(hsl0 + t2 * 512, hsl0 + t2 * 512 + 512)
                    tv = evac(ctx, "vector", psD,
                        lambda e2, oc=oc, asl=asl, psD=psD: e2.tensor_tensor(
                            yb[:, oc * T + asl.start: oc * T + asl.stop],
                            yb[:, oc * T + asl.start: oc * T + asl.stop],
                            psD.ap[:, :], op=ALU.add))
                    last = tv
                    he_used.append(tv)
                    xe_used.append(tv)
                dslab_used[didx] = last
    ctx.barrier_all()
    frees = [ab, *reversed(dslabs), *reversed(slabs), he]
    if xe is not None:
        frees.append(xe)
    frees.append(wb)
    ctx.sbfree(*frees)


def block_l1_ffn(ctx, c, W, xres, park, dbg=None):
    xch = lambda ch: xres[:, ch * T:(ch + 1) * T]
    yb = ctx.sbuf("l1f_yb", [128, NCH * T], F32)
    ych = lambda ch: yb[:, ch * T:(ch + 1) * T]
    ln_to(ctx, c, xch, ych)
    if dbg == "hs":
        ctx.barrier_all(); ctx.sbfree(xres)
        return yb
    for ch in range(NCH):
        ctx.dma("sync", park[ch * 128:(ch + 1) * 128, :], xch(ch))
    ctx.barrier_all()
    ctx.sbfree(xres)

    lncols = ctx.sbuf("l1f_lncols", [128, 16], F32)
    ctx.dma("sync", lncols[:, 0:8], W["L1f_lnw"])
    ctx.dma("sync", lncols[:, 8:16], W["L1f_lnb"])
    xln = ctx.sbuf("l1f_xln", [128, NCH * T], BF16)
    lch = lambda ch: xln[:, ch * T:(ch + 1) * T]
    ln_to(ctx, c, ych, lch,
          affine=(lambda ch: lncols[:, ch:ch + 1], lambda ch: lncols[:, 8 + ch:9 + ch]))

    rtw = ctx.sbuf("l1f_rtw", [128, 64], BF16)
    rt_tok = ctx.dma("gpsimd", rtw[:],
                     W["L1f_router"].rearrange("(kc p) e -> p kc e", p=128),
                     allow_slow_non_contiguous=True)
    scr = ctx.sbuf("l1f_scr", [128, 128], F32)
    wtm = ctx.sbuf("l1f_wtm", [128, 128], F32)
    sm = c["smalls"]
    ps0 = ctx.psum[0]
    sc_toks = []
    for tt in range(16):
        pairs = [(xln[:, kc * T + tt * 128: kc * T + tt * 128 + 128],
                  rtw[:, kc * 8:(kc + 1) * 8], [rt_tok]) for kc in range(NCH)]
        mm_group(ctx, ps0, ps0.ap[:, 0:8], pairs)
        sc_toks.append(evac(ctx, "vector", ps0,
            lambda e, tt=tt: e.tensor_copy(scr[:, tt * 8:(tt + 1) * 8], ps0.ap[:, 0:8])))
    wtm_toks = []
    for tt in range(16):
        ssl = slice(tt * 8, (tt + 1) * 8)
        t1 = ctx.op("vector", lambda e, ssl=ssl: e.reduce_max(
            sm[:, 0:1], scr[:, ssl], axis=AX.X), deps=[sc_toks[tt]], sig=True)
        t2 = ctx.op("scalar", lambda e: e.activation(
            sm[:, 1:2], sm[:, 0:1], AF.Sigmoid, scale=1.0), deps=[t1], sig=True)
        t3 = ctx.op("vector", lambda e, ssl=ssl: e.tensor_scalar(
            wtm[:, ssl], scr[:, ssl], sm[:, 0:1], sm[:, 1:2], ALU.is_equal, ALU.mult),
            deps=[t2], sig=True)
        wtm_toks.append(t3)
    wfm = ctx.sbuf("l1f_wfm", [8, T], BF16)
    ps4 = ctx.psum[4]
    wfm_toks = []
    for tt in range(16):
        transpose128(ctx, ps4, ps4.ap[0:8, 0:128], wtm[:, tt * 8:(tt + 1) * 8],
                     c["c_ident"][:], deps=[wtm_toks[tt]])
        wfm_toks.append(evac(ctx, "vector", ps4,
            lambda e, tt=tt: e.tensor_copy(wfm[0:8, tt * 128:(tt + 1) * 128],
                                           ps4.ap[0:8, 0:128])))
    if dbg == "wb":
        pass

    gate3 = W["L1f_gate"].rearrange("e (kc p) m -> e p kc m", p=128)
    up3 = W["L1f_up"].rearrange("e (kc p) m -> e p kc m", p=128)
    down3 = W["L1f_down"].rearrange("e (mc p) o -> e p mc o", p=128)
    shg3 = W["L1f_shgate"].rearrange("(kc p) m -> p kc m", p=128)
    shu3 = W["L1f_shup"].rearrange("(kc p) m -> p kc m", p=128)
    shd3 = W["L1f_shdown"].rearrange("(mc p) o -> p mc o", p=128)
    wspec = [(gate3[e], up3[e], down3[e], e) for e in range(8)]
    wspec.append((shg3, shu3, shd3, None))
    moe_experts(ctx, c, "l1f", yb, xln, wfm, wfm_toks, wspec, mode="input")
    ctx.barrier_all()
    ctx.sbfree(wfm, wtm, scr, rtw, xln, lncols)

    xres2 = ctx.sbuf("xres", [128, NCH * T], F32, side="right")
    rtoks = [ctx.dma("sync", xres2[:, ch * T:(ch + 1) * T],
                     park[ch * 128:(ch + 1) * 128, :]) for ch in range(NCH)]
    x2ch = lambda ch: xres2[:, ch * T:(ch + 1) * T]
    ln_to(ctx, c, ych, x2ch, accumulate=True, acc_dst=x2ch, deps=rtoks)
    ctx.barrier_all()
    ctx.sbfree(yb)
    return xres2



def block_l0_ffn(ctx, c, W, xres, park, dbg=None):
    xch = lambda ch: xres[:, ch * T:(ch + 1) * T]
    lnsq_b = ctx.sbuf("l0f_lnsq", [128, 2048], BF16)
    lntmp_b = ctx.sbuf("l0f_lntmp", [128, 1024], F32)
    yb = ctx.sbuf("l0f_yb", [128, NCH * T], F32)
    ych = lambda ch: yb[:, ch * T:(ch + 1) * T]
    ln_to(ctx, c, xch, ych, sq_buf=lnsq_b, tmp_buf=lntmp_b)
    for ch in range(NCH):
        ctx.dma("sync", park[ch * 128:(ch + 1) * 128, :], xch(ch))
    ctx.barrier_all()
    ctx.sbfree(xres)
    if dbg == "h":
        return yb
    xln = ctx.sbuf("l0f_xln", [128, NCH * T], BF16)
    lch = lambda ch: xln[:, ch * T:(ch + 1) * T]
    lncols = ctx.sbuf("l0f_lncols", [128, 32], F32)
    ctx.dma("sync", lncols[:], W["L0f_lncols"])
    ctx.barrier_all()
    ln_to(ctx, c, ych, lch, sq_buf=lnsq_b, tmp_buf=lntmp_b,
          affine=(lambda ch: lncols[:, ch:ch + 1], lambda ch: lncols[:, 8 + ch:9 + ch]))
    attn = ctx.sbuf("l0f_attn", [128, NCH * T], BF16)
    std_attention(ctx, c, xln, attn, wqkv=W["L0f_qkv"], bqkv=None, rope_freq=None,
                  causal=True, scale=0.125, tag="l0f",
                  dbg=dbg if dbg in ("qtm", "ktm", "vtm") else None)
    if dbg in ("qtm", "ktm", "vtm", "attnraw"):
        for ch in range(NCH):
            ctx.op("vector", lambda e, ch=ch: e.tensor_copy(
                ych(ch), attn[:, ch * T:(ch + 1) * T]), sig=True)
        ctx.barrier_all()
        ctx.sbfree(attn, lncols, xln)
        return yb
    bo = ctx.sbuf("l0f_bo", [128, 8], F32)
    bt = ctx.dma("sync", bo[:], W["L0f_bo"])
    wo3 = W["L0f_wo"].rearrange("(kc p) n -> p kc n", p=128)
    wslab = ctx.sbuf("l0f_wsl", [128, 1024], BF16)
    wsl_used = None
    for mch in range(NCH):
        wt = ctx.dma("gpsimd", wslab[:], wo3[:, :, mch * 128:(mch + 1) * 128],
                     deps=[wsl_used])
        for n in range(NT):
            ps = ctx.psum[n % 2]
            pairs = [(wslab[:, kc * 128:(kc + 1) * 128],
                      attn[:, kc * T + n * 512: kc * T + n * 512 + 512], [wt])
                     for kc in range(NCH)]
            mm_group(ctx, ps, ps.ap[:, :], pairs)
            wsl_used = evac(ctx, "vector", ps, lambda e, mch=mch, n=n, ps=ps: e.scalar_tensor_tensor(
                ych(mch)[:, n * 512:(n + 1) * 512], ps.ap[:, :], bo[:, mch:mch + 1],
                ych(mch)[:, n * 512:(n + 1) * 512], ALU.add, ALU.add), deps=[bt])
    ctx.barrier_all()
    ctx.sbfree(wslab, bo, attn)
    if dbg == "attnout":
        ctx.sbfree(lncols, xln)
        return yb
    ln_to(ctx, c, ych, lch, sq_buf=lnsq_b, tmp_buf=lntmp_b,
          affine=(lambda ch: lncols[:, 16 + ch:17 + ch], lambda ch: lncols[:, 24 + ch:25 + ch]))
    rtw = ctx.sbuf("l0f_rtw", [128, 64], BF16)
    rt_tok = ctx.dma("gpsimd", rtw[:],
                     W["L0f_router"].rearrange("(kc p) e -> p kc e", p=128),
                     allow_slow_non_contiguous=True)
    scr = ctx.sbuf("l0f_scr", [128, 128], F32)
    scr2 = ctx.sbuf("l0f_scr2", [128, 8], F32)
    wtm = ctx.sbuf("l0f_wtm", [128, 128], F32)
    wt2 = ctx.sbuf("l0f_wt2", [128, 8], F32)
    sm = c["smalls"]
    ps0 = ctx.psum[0]
    sc_toks = []
    for tt in range(16):
        pairs = [(xln[:, kc * T + tt * 128: kc * T + tt * 128 + 128],
                  rtw[:, kc * 8:(kc + 1) * 8], [rt_tok]) for kc in range(NCH)]
        mm_group(ctx, ps0, ps0.ap[:, 0:8], pairs)
        sc_toks.append(evac(ctx, "vector", ps0,
            lambda e, tt=tt: e.tensor_copy(scr[:, tt * 8:(tt + 1) * 8], ps0.ap[:, 0:8])))
    wtm_toks = []
    prev = None
    for tt in range(16):
        ssl = slice(tt * 8, (tt + 1) * 8)
        t1 = ctx.op("vector", lambda e, ssl=ssl: e.reduce_max(
            sm[:, 0:1], scr[:, ssl], axis=AX.X), deps=[sc_toks[tt], prev], sig=True)
        t2 = ctx.op("vector", lambda e, ssl=ssl: e.tensor_scalar(
            scr2[:], scr[:, ssl], sm[:, 0:1], -1e30, ALU.is_equal, ALU.mult),
            deps=[t1], sig=True)
        t3 = ctx.op("vector", lambda e, ssl=ssl: e.tensor_tensor(
            scr2[:], scr[:, ssl], scr2[:], op=ALU.add), deps=[t2], sig=True)
        t4 = ctx.op("vector", lambda e: e.reduce_max(
            sm[:, 1:2], scr2[:], axis=AX.X), deps=[t3], sig=True)
        t5 = ctx.op("vector", lambda e: e.tensor_tensor(
            sm[:, 2:3], sm[:, 0:1], sm[:, 1:2], op=ALU.subtract), deps=[t4], sig=True)
        t6 = ctx.op("scalar", lambda e: e.activation(
            sm[:, 3:4], sm[:, 2:3], AF.Sigmoid, scale=1.0), deps=[t5], sig=True)
        t7 = ctx.op("vector", lambda e: e.tensor_scalar(
            sm[:, 4:5], sm[:, 3:4], -1.0, 1.0, ALU.mult, ALU.add), deps=[t6], sig=True)
        t8 = ctx.op("vector", lambda e, ssl=ssl: e.tensor_scalar(
            wtm[:, ssl], scr[:, ssl], sm[:, 0:1], sm[:, 3:4], ALU.is_equal, ALU.mult),
            deps=[t7], sig=True)
        t9 = ctx.op("vector", lambda e, ssl=ssl: e.tensor_scalar(
            wt2[:], scr2[:], sm[:, 1:2], sm[:, 4:5], ALU.is_equal, ALU.mult),
            deps=[t8], sig=True)
        t10 = ctx.op("vector", lambda e, ssl=ssl: e.tensor_tensor(
            wtm[:, ssl], wtm[:, ssl], wt2[:], op=ALU.add), deps=[t9], sig=True)
        wtm_toks.append(t10)
        prev = t10
    wfm = ctx.sbuf("l0f_wfm", [8, T], BF16)
    ps4 = ctx.psum[4]
    wfm_toks = []
    for tt in range(16):
        transpose128(ctx, ps4, ps4.ap[0:8, 0:128], wtm[:, tt * 8:(tt + 1) * 8],
                     c["c_ident"][:], deps=[wtm_toks[tt]])
        wfm_toks.append(evac(ctx, "vector", ps4,
            lambda e, tt=tt, ps4=ps4: e.tensor_copy(wfm[0:8, tt * 128:(tt + 1) * 128],
                                           ps4.ap[0:8, 0:128])))
    g3 = W["L0f_w1"].rearrange("e (kc p) m -> e p kc m", p=128)
    u3 = W["L0f_w2"].rearrange("e (kc p) m -> e p kc m", p=128)
    d3 = W["L0f_w3"].rearrange("e (mc p) o -> e p mc o", p=128)
    wspec = [(g3[e], u3[e], d3[e], e) for e in range(8)]
    moe_experts(ctx, c, "l0f", yb, xln, wfm, wfm_toks, wspec, mode="hidden")
    ctx.barrier_all()
    ctx.sbfree(wfm, wt2, wtm, scr2, scr, rtw, lncols, xln)
    xres2 = ctx.sbuf("xres", [128, NCH * T], F32, side="right")
    rtoks = [ctx.dma("sync", xres2[:, ch * T:(ch + 1) * T],
                     park[ch * 128:(ch + 1) * 128, :]) for ch in range(NCH)]
    x2ch = lambda ch: xres2[:, ch * T:(ch + 1) * T]
    ln_to(ctx, c, ych, x2ch, accumulate=True, acc_dst=x2ch, deps=rtoks,
          sq_buf=lnsq_b, tmp_buf=lntmp_b)
    ctx.barrier_all()
    ctx.sbfree(yb, lntmp_b, lnsq_b)
    return xres2


WEIGHT_SPECS = {
    "l1f": {
        "L1f_lnw": [128, 8], "L1f_lnb": [128, 8],
        "L1f_router": [1024, 8],
        "L1f_gate": [8, 1024, 2048], "L1f_up": [8, 1024, 2048],
        "L1f_down": [8, 2048, 1024],
        "L1f_shgate": [1024, 2048], "L1f_shup": [1024, 2048],
        "L1f_shdown": [2048, 1024],
    },
}

BLOCK_EMIT = {}


def make_nc(blocks, blkkw=None, nseq=2):
    nc = bass.Bass("TRN2", target_bir_lowering=False)
    dram = {}
    dram["x"] = nc.dram_tensor("x", [2, 1024, T], F32, kind="ExternalInput").ap()
    dram["out"] = nc.dram_tensor("out", [2, 1024, T], F32, kind="ExternalOutput").ap()
    park = nc.dram_tensor("park", [1024, T], F32, kind="Internal").ap()
    dram["park2"] = nc.dram_tensor("park2", [1024, T], BF16, kind="Internal").ap()
    for cname, (shape, dt) in CONST_SPECS.items():
        dram[cname] = nc.dram_tensor(cname, shape, dt, kind="ExternalInput").ap()
    for b in blocks:
        for wname, shape in WEIGHT_SPECS[b].items():
            dt = F32
            if isinstance(shape, tuple):
                shape, dts = shape
                dt = BF16 if dts == "bf16" else F32
            dram[wname] = nc.dram_tensor(wname, list(shape), dt, kind="ExternalInput").ap()
    es = ExitStack()
    with es:
        ctx = Ctx(nc, es)
        ctx.mkpsum()
        c = load_consts(ctx, dram)
        for seq in range(nseq):
            xres = ctx.sbuf("xres", [128, NCH * T], F32, side="right")
            for ch in range(NCH):
                ctx.dma("sync", xres[:, ch * T:(ch + 1) * T],
                        dram["x"][seq, ch * 128:(ch + 1) * 128, :])
            ctx.barrier_all()
            for b in blocks:
                xres = BLOCK_EMIT[b](ctx, c, dram, xres, park, **(blkkw or {}))
            for ch in range(NCH):
                ctx.dma("sync", dram["out"][seq, ch * 128:(ch + 1) * 128, :],
                        xres[:, ch * T:(ch + 1) * T])
            ctx.barrier_all()
            ctx.sbfree(xres)
        with nc.Block() as blk:
            ctx.flush(blk)
    return nc


def stage_weights(inputs, blocks):
    w = dict(build_consts())
    f32 = np.float32
    if "l0a" in blocks:
        qw = np.asarray(inputs["L0_attn_qkv_w"], f32)
        w["L0a_qkv"] = _bf(np.concatenate([qw[0].T, qw[1].T, qw[2].T], axis=1))
        w["L0a_cproj"] = _bf(np.asarray(inputs["L0_attn_c_proj_w"], f32).T)
        w["L0a_kfc"] = _bf(np.asarray(inputs["L0_attn_k_fc_w"], f32).T)
        w["L0a_kpj"] = _bf(np.asarray(inputs["L0_attn_k_pj_w"], f32).T)
        w["L0a_vfc"] = _bf(np.asarray(inputs["L0_attn_v_fc_w"], f32).T)
        w["L0a_vpj"] = _bf(np.asarray(inputs["L0_attn_v_pj_w"], f32).T)
        w["L0a_strat"] = _bf(np.asarray(inputs["L0_attn_strat_w"], f32).T)
        w["L0a_stratb"] = _bf(np.asarray(inputs["L0_attn_strat_b"], f32).reshape(24, 1))
        w["L0a_kpos"] = _bf(np.asarray(inputs["L0_attn_k_pos"], f32).transpose(2, 0, 1).reshape(128, 128))
        w["L0a_vpos"] = _bf(np.asarray(inputs["L0_attn_v_pos"], f32).transpose(2, 0, 1).reshape(128, 128))
        mem = np.asarray(inputs["L0_attn_mem_kv"], f32)
        w["L0a_memk"] = _bf(mem[0, :, 0, :].T)
        w["L0a_memv"] = _bf(mem[1, :, 0, :].reshape(1, 1024))
        w["L0a_lam"] = _bf(np.asarray(inputs["L0_attn_lambdas"], f32).reshape(1, 2))
        n_ = 32
        ang = (np.float32(1.0) / np.float32(1024.0)) ** np.linspace(0, 1, n_, dtype=np.float32)
        ang = np.concatenate([ang, np.zeros(32, np.float32)])
        th = np.arange(2048, dtype=np.float32)[:, None] * ang[None, :]
        cosq = np.cos(th.astype(np.float64)).astype(np.float32)
        sinq = np.sin(th.astype(np.float64)).astype(np.float32)
        cos_rep = np.tile(cosq, (1, 4))
        sin_rep = np.tile(sinq, (1, 4))
        w["L0a_rot"] = _bf(np.concatenate([cos_rep, sin_rep], axis=1))
        et = np.zeros((128, 2048), np.float32)
        for kt in range(16):
            for k in range(128):
                et[kt * 8 + k // 16, kt * 128 + k] = 1.0
        sbm_ = np.zeros((128, 2048), np.float32)
        for j in range(4):
            for k in range(128):
                for q in range(512):
                    if 8 * j + k // 16 == q // 16:
                        sbm_[k, j * 512 + q] = 1.0
        slm_ = np.zeros((128, 2560), np.float32)
        for ji, j in enumerate((-1, 0, 1, 2, 3)):
            d = (np.arange(512)[None, :] - np.arange(128)[:, None]) - 128 * j
            slm_[:, ji * 512:(ji + 1) * 512] = ((d >= 0) & (d < 128)).astype(np.float32)
        w["L0a_et"] = et.astype(ml_dtypes.bfloat16)
        w["L0a_sbm"] = sbm_.astype(ml_dtypes.bfloat16)
        w["L0a_slm"] = slm_.astype(ml_dtypes.bfloat16)
    if "l0f" in blocks:
        w["L0f_qkv"] = _bf(np.concatenate([inputs["L0_ffn_wq"], inputs["L0_ffn_wk"],
                                           inputs["L0_ffn_wv"]], axis=1))
        w["L0f_wo"] = _bf(inputs["L0_ffn_wo"])
        w["L0f_bo"] = _bf(np.asarray(inputs["L0_ffn_bo"], f32).reshape(8, 128).T)
        w["L0f_lncols"] = _bf(np.concatenate([
            np.asarray(inputs["L0_ffn_ln1_scale"], f32).reshape(8, 128).T,
            np.asarray(inputs["L0_ffn_ln1_shift"], f32).reshape(8, 128).T,
            np.asarray(inputs["L0_ffn_ln2_scale"], f32).reshape(8, 128).T,
            np.asarray(inputs["L0_ffn_ln2_shift"], f32).reshape(8, 128).T], axis=1))
        w["L0f_router"] = _bf(inputs["L0_ffn_gate_w"])
        w["L0f_w1"] = _bf(inputs["L0_ffn_w1"])
        w["L0f_w2"] = _bf(inputs["L0_ffn_w2"])
        w["L0f_w3"] = _bf(inputs["L0_ffn_w3"])
    if "l1a" in blocks:
        w["L1a_wqkv"] = _bf(inputs["L1_attn_w_qkv"])
        w["L1a_bqkv"] = _bf(np.asarray(inputs["L1_attn_b_qkv"], f32).reshape(1, 3072))
        w["L1a_wout"] = _bf(inputs["L1_attn_w_out"])
        w["L1a_bout"] = _bf(np.asarray(inputs["L1_attn_b_out"], f32).reshape(8, 128).T)
        w["L1a_freq"] = _bf(np.asarray(inputs["L1_attn_rope_freqs"], f32).reshape(1, 512))
    if "l1f" in blocks:
        w["L1f_lnw"] = _bf(np.asarray(inputs["L1_ffn_ln_w"], f32).reshape(8, 128).T)
        w["L1f_lnb"] = _bf(np.asarray(inputs["L1_ffn_ln_b"], f32).reshape(8, 128).T)
        w["L1f_router"] = _bf(inputs["L1_ffn_router_w"])
        w["L1f_gate"] = _bf(inputs["L1_ffn_gate_w"])
        w["L1f_up"] = _bf(inputs["L1_ffn_up_w"])
        w["L1f_down"] = _bf(inputs["L1_ffn_down_w"])
        w["L1f_shgate"] = _bf(inputs["L1_ffn_sh_gate"])
        w["L1f_shup"] = _bf(inputs["L1_ffn_sh_up"])
        w["L1f_shdown"] = _bf(inputs["L1_ffn_sh_down"])
    return w


def _bf(x):
    return np.ascontiguousarray(x, dtype=np.float32)


def run_blocks(x_full, inputs, blocks, n_cores=8, blkkw=None, nseq=2):
    B = x_full.shape[0]
    assert B == 2 * n_cores
    w = stage_weights(inputs, blocks)
    nc = make_nc(blocks, blkkw=blkkw, nseq=nseq)
    in_maps = []
    for cc in range(n_cores):
        xs = np.ascontiguousarray(x_full[2 * cc:2 * cc + 2].transpose(0, 2, 1))
        m = {"x": xs}
        m.update(w)
        in_maps.append(m)
    res = run_bass_kernel_spmd(nc, in_maps, core_ids=list(range(n_cores)))
    outs = [r["out"].transpose(0, 2, 1) for r in res.results]
    return np.concatenate(outs, axis=0)


BLOCK_EMIT["l1f"] = block_l1_ffn


def build_cmask(ctx, c):
    if "cmask" in c:
        return c["cmask"]
    cm = ctx.sbuf("cmask", [128, 4 * 512], BF16)
    io = ctx.sbuf("cmio", [128, 512], F32)
    t0 = ctx.op("gpsimd", lambda e: e.iota(io[:], pattern=[[-1, 512]], base=0,
                channel_multiplier=1, allow_small_or_imprecise_dtypes=True), sig=True)
    for j in range(4):
        ctx.op("vector", lambda e, j=j: e.tensor_scalar(
            cm[:, j * 512:(j + 1) * 512], io[:], float(-128 * j), None,
            ALU.is_le, ALU.bypass), deps=[t0], sig=True)
    c["cmask"] = cm
    c["cmask_tok"] = ctx.last["vector"]
    ctx.barrier_all()
    ctx.sbfree(io)
    return cm


def std_attention(ctx, c, xln, attn, *, wqkv, bqkv=None, rope_freq=None,
                  causal=False, scale, n_heads=16, negC_col=5, tag="att", dbg=None):
    WH = 4
    n_waves = n_heads // WH
    cmask = build_cmask(ctx, c) if causal else None
    w3 = wqkv.rearrange("(kc p) n -> p kc n", p=128)
    for w in range(n_waves):
        qtm = ctx.sbuf(tag + "qtm", [128, 4096], BF16)
        ktm = ctx.sbuf(tag + "ktm", [128, 4096], BF16)
        vtm = ctx.sbuf(tag + "vtm", [128, 4096], BF16)
        slabs = [ctx.sbuf(tag + f"sl{j}", [128, 2048], BF16) for j in range(3)]
        stoks = []
        for j in range(3):
            col0 = j * 1024 + w * 256
            stoks.append(ctx.dma("gpsimd", slabs[j][:], w3[:, :, col0:col0 + 256]))
        btiles = None
        btoks = []
        if bqkv is not None:
            brow = ctx.sbuf(tag + "brow", [1, 768], F32)
            btiles = ctx.sbuf(tag + "btile", [128, 768], F32)
            for j in range(3):
                bt = ctx.dma("sync", brow[0:1, j * 256:(j + 1) * 256],
                             bqkv[0:1, j * 1024 + w * 256: j * 1024 + w * 256 + 256])
                ps = ctx.psum[4]
                mm(ctx, ps, ps.ap[:, 0:256], c["c_onesrow"][:],
                   brow[0:1, j * 256:(j + 1) * 256], start=True, stop=True, deps=[bt])
                btoks.append(evac(ctx, "vector", ps, lambda e, j=j, ps=ps: e.tensor_copy(
                    btiles[:, j * 256:(j + 1) * 256], ps.ap[:, 0:256])))
        dsts = (qtm, ktm, vtm)
        proj_toks = {0: [], 1: [], 2: []}
        for tt in range(16):
            for j in range(3):
                ps = ctx.psum[j % 2]
                pairs = [(xln[:, kc * T + tt * 128: kc * T + tt * 128 + 128],
                          slabs[j][:, kc * 256:(kc + 1) * 256], [stoks[j]])
                         for kc in range(NCH)]
                mm_group(ctx, ps, ps.ap[:, 0:256], pairs)
                osl = slice(tt * 256, (tt + 1) * 256)
                if btiles is not None:
                    t = evac(ctx, "vector", ps, lambda e, j=j, osl=osl, ps=ps: e.tensor_tensor(
                        dsts[j][:, osl], ps.ap[:, 0:256], btiles[:, j * 256:(j + 1) * 256],
                        op=ALU.add), deps=btoks)
                else:
                    t = evac(ctx, "vector", ps, lambda e, j=j, osl=osl, ps=ps: e.tensor_copy(
                        dsts[j][:, osl], ps.ap[:, 0:256]))
                proj_toks[j].append(t)
        ctx.barrier_all()
        if btiles is not None:
            ctx.sbfree(btiles, brow)
        ctx.sbfree(*reversed(slabs))
        if rope_freq is not None:
            frow = ctx.sbuf(tag + "frow", [1, 128], F32)
            fbc = ctx.sbuf(tag + "fbc", [128, 128], F32)
            rtmp = ctx.sbuf(tag + "rtmp", [128, 128 * 9], F32)
            ft = ctx.dma("sync", frow[:], rope_freq[0:1, w * 128:(w + 1) * 128])
            ps = ctx.psum[4]
            mm(ctx, ps, ps.ap[:, 0:128], c["c_onesrow"][:], frow[:],
               start=True, stop=True, deps=[ft])
            fbt = evac(ctx, "vector", ps, lambda e, ps=ps: e.tensor_copy(fbc[:], ps.ap[:, 0:128]))
            tcol = rtmp[:, 1024:1025]
            th = lambda: rtmp[:, 0:128]
            uu = lambda: rtmp[:, 128:256]
            rr = lambda: rtmp[:, 256:384]
            sint = lambda: rtmp[:, 384:512]
            cost = lambda: rtmp[:, 512:640]
            prev = [None, None]
            for tt in range(16):
                i0 = ctx.op("gpsimd", lambda e, tt=tt: e.iota(
                    tcol, pattern=[[1, 1]], base=tt * 128, channel_multiplier=1,
                    allow_small_or_imprecise_dtypes=True), deps=prev, sig=True)
                t1 = ctx.op("vector", lambda e: e.tensor_scalar(
                    th(), fbc[:], tcol, None, ALU.mult, ALU.bypass), deps=[i0, fbt], sig=True)
                t2 = ctx.op("vector", lambda e: e.tensor_scalar(
                    uu(), th(), 1.0 / TWO_PI, 2.0 ** 23, ALU.mult, ALU.add), deps=[t1], sig=True)
                t3 = ctx.op("vector", lambda e: e.tensor_scalar(
                    uu(), uu(), -(2.0 ** 23), None, ALU.add, ALU.bypass), deps=[t2], sig=True)
                t4 = ctx.op("vector", lambda e: e.scalar_tensor_tensor(
                    rr(), uu(), -CW_C1, th(), ALU.mult, ALU.add), deps=[t3], sig=True)
                t5 = ctx.op("vector", lambda e: e.scalar_tensor_tensor(
                    rr(), uu(), -CW_C2, rr(), ALU.mult, ALU.add), deps=[t4], sig=True)
                s1 = ctx.op("scalar", lambda e: e.activation(
                    sint(), rr(), AF.Sin, scale=1.0), deps=[t5], sig=True)
                s2 = ctx.op("scalar", lambda e: e.activation(
                    cost(), rr(), AF.Abs, scale=1.0), deps=[t5], sig=True)
                s3 = ctx.op("scalar", lambda e: e.activation(
                    cost(), cost(), AF.Sin, bias=c["c_cols"][:, 0:1], scale=-1.0),
                    deps=[s2], sig=True)
                cos3 = cost().rearrange("p (h f) -> p h f", h=4)
                sin3 = sint().rearrange("p (h f) -> p h f", h=4)
                la = rtmp[:, 640:768].rearrange("p (h f) -> p h f", h=4)
                lb = rtmp[:, 768:896].rearrange("p (h f) -> p h f", h=4)
                prev = []
                for j, dst in ((0, qtm), (1, ktm)):
                    v3 = dst[:, tt * 256:(tt + 1) * 256].rearrange("p (h d) -> p h d", h=4)
                    lo = v3[:, :, 0:32]
                    hi = v3[:, :, 32:64]
                    d0 = [s1, s3, proj_toks[j][tt]]
                    b1 = ctx.op("vector", lambda e, lo=lo, la=la, cos3=cos3: e.tensor_tensor(la, lo, cos3, op=ALU.mult), deps=d0, sig=True)
                    b2 = ctx.op("vector", lambda e, hi=hi, lb=lb, sin3=sin3: e.tensor_tensor(lb, hi, sin3, op=ALU.mult), deps=d0, sig=True)
                    b3 = ctx.op("vector", lambda e, lo=lo, sin3=sin3: e.tensor_tensor(lo, lo, sin3, op=ALU.mult), deps=[b1], sig=True)
                    b4 = ctx.op("vector", lambda e, hi=hi, cos3=cos3: e.tensor_tensor(hi, hi, cos3, op=ALU.mult), deps=[b2], sig=True)
                    b5 = ctx.op("vector", lambda e, la=la, lb=lb: e.tensor_tensor(la, la, lb, op=ALU.subtract), deps=[b3, b4], sig=True)
                    b6 = ctx.op("vector", lambda e, lo=lo, hi=hi: e.tensor_tensor(hi, lo, hi, op=ALU.add), deps=[b5], sig=True)
                    b7 = ctx.op("vector", lambda e, lo=lo, la=la: e.tensor_copy(lo, la), deps=[b6], sig=True)
                    prev += [b6, b7]
                    proj_toks[j][tt] = b7
            ctx.barrier_all()
            ctx.sbfree(rtmp, fbc, frow)
        if dbg in ("qtm", "ktm", "vtm"):
            srcb = {"qtm": qtm, "ktm": ktm, "vtm": vtm}[dbg]
            ctx.op("vector", lambda e, srcb=srcb, w=w: e.tensor_copy(
                attn[:, w * 4096:(w + 1) * 4096], srcb[:]), sig=True)
            ctx.barrier_all()
            ctx.sbfree(vtm, ktm, qtm)
            continue
        qfm = ctx.sbuf(tag + "qfm", [128, 4096], BF16)
        kfm = ctx.sbuf(tag + "kfm", [128, 4096], BF16)
        tr_toks = []
        for j, (src, dst) in enumerate(((qtm, qfm), (ktm, kfm))):
            for tt in range(16):
                for chh in range(2):
                    psb = ctx.psum[6 + (tt + chh) % 2]
                    transpose128(ctx, psb, psb.ap[0:128, 0:128],
                                 src[:, tt * 256 + chh * 128: tt * 256 + chh * 128 + 128],
                                 c["c_ident_bf"][:], deps=[proj_toks[j][tt]])
                    tr_toks.append(evac(ctx, "vector", psb,
                        lambda e, dst=dst, chh=chh, tt=tt, psb=psb: e.tensor_copy(
                            dst[:, chh * 2048 + tt * 128: chh * 2048 + tt * 128 + 128],
                            psb.ap[0:128, 0:128])))
        ptile = ctx.sbuf(tag + "pt", [128, 1024], BF16)
        stile = ctx.sbuf(tag + "st", [128, 512], F32)
        arow = ctx.sbuf(tag + "ar", [1, 512], F32)
        vdep = proj_toks[2]
        for hh in range(WH):
            hg = w * WH + hh
            base = (hh % 2) * 64
            chh = hh // 2
            for qt in range(4):
                q_rhs = qfm[base:base + 64, chh * 2048 + qt * 512: chh * 2048 + qt * 512 + 512]
                kts = list(range(4 * qt + 4)) if causal else list(range(16))
                psAV = ctx.psum[2]
                psSum = ctx.psum[3]
                for ki, kt in enumerate(kts):
                    psS = ctx.psum[ki % 2]
                    lh_k = kfm[base:base + 64, chh * 2048 + kt * 128: chh * 2048 + kt * 128 + 128]
                    mm(ctx, psS, psS.ap[0:128, 0:512], lh_k, q_rhs, start=True,
                       stop=True, deps=tr_toks if ki < 2 else ())
                    pt = ptile[:, (ki % 2) * 512:(ki % 2) * 512 + 512]
                    te = evac(ctx, "scalar", psS, lambda e, pt=pt, psS=psS: e.activation(
                        pt, psS.ap[0:128, 0:512], AF.Exp, scale=scale,
                        bias=c["c_cols"][:, negC_col:negC_col + 1]))
                    if causal and kt >= 4 * qt:
                        j = kt - 4 * qt
                        te = ctx.op("vector", lambda e, pt=pt, j=j: e.tensor_tensor(
                            pt, pt, cmask[:, j * 512:(j + 1) * 512], op=ALU.mult),
                            deps=[te, c["cmask_tok"]], sig=True)
                        ctx.psum[ki % 2].tok_r = te
                    lh_v = vtm[:, kt * 256 + hh * 64: kt * 256 + hh * 64 + 64]
                    mm(ctx, psAV, psAV.ap[0:64, 0:512], lh_v, pt,
                       start=(ki == 0), stop=(ki == len(kts) - 1),
                       deps=[te] + (vdep if ki < 2 else []))
                    mm(ctx, psSum, psSum.ap[0:1, 0:512], c["c_onescol_bf"][:], pt,
                       start=(ki == 0), stop=(ki == len(kts) - 1), deps=[te])
                tr = evac(ctx, "vector", psSum, lambda e, psSum=psSum: e.reciprocal(
                    arow[:], psSum.ap[0:1, 0:512]))
                psB = ctx.psum[4]
                mm(ctx, psB, psB.ap[0:64, 0:512], c["c_onesrow"][0:1, 0:64], arow[:],
                   start=True, stop=True, deps=[tr])
                ts_ = evac(ctx, "vector", psAV, lambda e, psAV=psAV: e.tensor_copy(
                    stile[0:64, 0:512], psAV.ap[0:64, 0:512]))
                o0 = (hg // 2) * T + qt * 512
                evac(ctx, "vector", psB, lambda e, base=base, o0=o0, psB=psB: e.tensor_tensor(
                    attn[base:base + 64, o0:o0 + 512],
                    stile[0:64, 0:512], psB.ap[0:64, 0:512], op=ALU.mult), deps=[ts_])
        ctx.barrier_all()
        ctx.sbfree(arow, stile, ptile, kfm, qfm, vtm, ktm, qtm)



def ych2(yb, ch):
    return yb[:, ch * T:(ch + 1) * T]


def block_l1_attn(ctx, c, W, xres, park, dbg=None):
    xch = lambda ch: xres[:, ch * T:(ch + 1) * T]
    attn = ctx.sbuf("l1a_attn", [128, NCH * T], BF16)
    xln = ctx.sbuf("l1a_xln", [128, NCH * T], BF16)
    lch = lambda ch: xln[:, ch * T:(ch + 1) * T]
    ln_to(ctx, c, xch, lch, cast_buf=attn)
    for ch in range(NCH):
        ctx.dma("sync", park[ch * 128:(ch + 1) * 128, :], xch(ch))
    ctx.barrier_all()
    ctx.sbfree(xres)
    lnsq_b = ctx.sbuf("l1a_lnsq", [128, 2048], BF16)
    lntmp_b = ctx.sbuf("l1a_lntmp", [128, 1024], F32)
    yb = ctx.sbuf("l1a_yb", [128, NCH * T], BF16)
    if dbg == "xln":
        for ch in range(NCH):
            ctx.op("vector", lambda e, ch=ch: e.tensor_copy(ych2(yb, ch), lch(ch)), sig=True)
        ctx.barrier_all()
        return yb
    std_attention(ctx, c, xln, attn, wqkv=W["L1a_wqkv"], bqkv=W["L1a_bqkv"],
                  rope_freq=W["L1a_freq"], causal=False, scale=0.125, tag="l1a",
                  dbg=dbg if dbg in ("qtm", "ktm", "vtm") else None)
    ctx.barrier_all()
    if dbg in ("attn", "qtm", "ktm", "vtm"):
        for ch in range(NCH):
            ctx.op("vector", lambda e, ch=ch: e.tensor_copy(
                yb[:, ch * T:(ch + 1) * T], attn[:, ch * T:(ch + 1) * T]), sig=True)
        ctx.barrier_all()
        return yb
    ych = lambda ch: yb[:, ch * T:(ch + 1) * T]
    bout = ctx.sbuf("l1a_bout", [128, 8], F32)
    bt = ctx.dma("sync", bout[:], W["L1a_bout"])
    wo3 = W["L1a_wout"].rearrange("(kc p) n -> p kc n", p=128)
    wslab = ctx.sbuf("l1a_wsl", [128, 1024], BF16)
    wsl_used = None
    for mch in range(NCH):
        wt = ctx.dma("gpsimd", wslab[:], wo3[:, :, mch * 128:(mch + 1) * 128],
                     deps=[wsl_used])
        for n in range(NT):
            ps = ctx.psum[n % 2]
            pairs = [(wslab[:, kc * 128:(kc + 1) * 128],
                      attn[:, kc * T + n * 512: kc * T + n * 512 + 512], [wt])
                     for kc in range(NCH)]
            mm_group(ctx, ps, ps.ap[:, :], pairs)
            wsl_used = evac(ctx, "scalar", ps, lambda e, mch=mch, n=n, ps=ps: e.activation(
                ych(mch)[:, n * 512:(n + 1) * 512], ps.ap[:, :], AF.Identity,
                bias=bout[:, mch:mch + 1], scale=1.0), deps=[bt])
    ctx.barrier_all()
    ctx.sbfree(wslab, bout)
    xres2 = ctx.sbuf("xres", [128, NCH * T], F32, side="right")
    rtoks = [ctx.dma("sync", xres2[:, ch * T:(ch + 1) * T],
                     park[ch * 128:(ch + 1) * 128, :]) for ch in range(NCH)]
    x2ch = lambda ch: xres2[:, ch * T:(ch + 1) * T]
    ln_to(ctx, c, ych, x2ch, accumulate=True, acc_dst=x2ch, deps=rtoks, src_f32=False,
          sq_buf=lnsq_b, tmp_buf=lntmp_b)
    ctx.barrier_all()
    ctx.sbfree(yb, lntmp_b, lnsq_b, xln, attn)
    return xres2


WEIGHT_SPECS["l0f"] = {
    "L0f_qkv": [1024, 3072], "L0f_wo": [1024, 1024], "L0f_bo": [128, 8],
    "L0f_lncols": [128, 32], "L0f_router": [1024, 8],
    "L0f_w1": [8, 1024, 2048], "L0f_w2": [8, 1024, 2048], "L0f_w3": [8, 2048, 1024],
}
BLOCK_EMIT["l0f"] = block_l0_ffn
WEIGHT_SPECS["l1a"] = {
    "L1a_wqkv": [1024, 3072], "L1a_bqkv": [1, 3072],
    "L1a_wout": [1024, 1024], "L1a_bout": [128, 8],
    "L1a_freq": [1, 512],
}
BLOCK_EMIT["l1a"] = block_l1_attn



def _l0a_masks(ctx, c, tag):
    cmc = ctx.sbuf(tag + "_cmc", [128, 2048], BF16)
    ramp = ctx.sbuf(tag + "_ramp", [128, 512], F32)
    io = ctx.sbuf(tag + "_io", [128, 512], F32)
    t0 = ctx.op("gpsimd", lambda e: e.iota(io[:], pattern=[[-1, 512]], base=0,
                channel_multiplier=16, allow_small_or_imprecise_dtypes=True), sig=True)
    toks = []
    for qt in range(4):
        toks.append(ctx.op("vector", lambda e, qt=qt: e.tensor_scalar(
            cmc[:, qt * 512:(qt + 1) * 512], io[:], float(qt * 512 - 16), None,
            ALU.is_le, ALU.bypass), deps=[t0], sig=True))
    t1 = ctx.op("gpsimd", lambda e: e.iota(io[:, 0:1], pattern=[[1, 1]], base=0,
                channel_multiplier=1, allow_small_or_imprecise_dtypes=True),
                deps=toks, sig=True)
    t2 = ctx.op("vector", lambda e: e.tensor_scalar(
        io[:, 1:2], io[:, 0:1], -1.0, -1000.0, ALU.mult, ALU.add), deps=[t1], sig=True)
    t3 = ctx.op("gpsimd", lambda e: e.memset(ramp[:], 1.0), deps=[t1], sig=True)
    t4 = ctx.op("vector", lambda e: e.tensor_scalar(
        ramp[:], ramp[:], io[:, 1:2], None, ALU.mult, ALU.bypass), deps=[t2, t3], sig=True)
    ctx.barrier_all()
    ctx.sbfree(io)
    return cmc, ramp, toks + [t4]


def block_l0_attn(ctx, c, W, xres, park, dbg=None):
    park2 = W["park2"]
    xch = lambda ch: xres[:, ch * T:(ch + 1) * T]
    lnsq_b = ctx.sbuf("l0a_lnsq", [128, 2048], BF16)
    lntmp_b = ctx.sbuf("l0a_lntmp", [128, 1024], F32)
    gfm = ctx.sbuf("l0a_gfm", [24, T], BF16)
    sw = ctx.sbuf("l0a_sw", [128, NCH * 24], BF16)
    sbc = ctx.sbuf("l0a_sbc", [24, 1], F32)
    lam = ctx.sbuf("l0a_lam", [1, 2], F32)
    xln = ctx.sbuf("l0a_xln", [128, NCH * T], BF16)
    lch = lambda ch: xln[:, ch * T:(ch + 1) * T]
    ln_to(ctx, c, xch, lch, sq_buf=lnsq_b, tmp_buf=lntmp_b)
    for ch in range(NCH):
        ctx.dma("sync", park[ch * 128:(ch + 1) * 128, :], xch(ch))
    ctx.barrier_all()
    ctx.sbfree(xres)

    sm = c["smalls"]
    t_sw = ctx.dma("gpsimd", sw[:], W["L0a_strat"].rearrange("(kc p) n -> p kc n", p=128))
    t_sb = ctx.dma("sync", sbc[:], W["L0a_stratb"])
    gf_toks = []
    for n in range(NT):
        ps = ctx.psum[0]
        pairs = [(sw[:, kc * 24:(kc + 1) * 24], lch(kc)[:, n * 512:(n + 1) * 512], [t_sw])
                 for kc in range(NCH)]
        mm_group(ctx, ps, ps.ap[0:24, :], pairs)
        gf_toks.append(evac(ctx, "scalar", ps, lambda e, n=n, ps=ps: e.activation(
            gfm[0:24, n * 512:(n + 1) * 512], ps.ap[0:24, :], AF.Sigmoid,
            bias=sbc[:], scale=1.0), deps=[t_sb]))
    t_lam = ctx.dma("sync", lam[:], W["L0a_lam"])
    ps = ctx.psum[1]
    mm(ctx, ps, ps.ap[:, 0:1], c["c_onesrow"][:], lam[0:1, 0:1], start=True, stop=True,
       deps=[t_lam])
    lamc = sm[:, 8:9]
    t_lamc = evac(ctx, "vector", ps, lambda e, ps=ps: e.tensor_copy(lamc, ps.ap[:, 0:1]))

    for ch in range(NCH):
        ctx.dma("sync", park2[ch * 128:(ch + 1) * 128, :], lch(ch))
    ctx.barrier_all()
    ctx.sbfree(xln)
    attn = ctx.sbuf("l0a_attn", [128, NCH * T], BF16)
    qfm = ctx.sbuf("l0a_qfm", [128, NCH * T], BF16)
    kfm = ctx.sbuf("l0a_kfm", [128, NCH * T], BF16)
    vtm = ctx.sbuf("l0a_vtm", [128, NCH * T], BF16)
    xlt = ctx.sbuf("l0a_xlt", [128, 2048], BF16)
    xlp = park2.rearrange("(kc p) t -> p kc t", p=128)
    qkv3 = W["L0a_qkv"].rearrange("(kc p) n -> p kc n", p=128)
    slab = ctx.sbuf("l0a_slab", [128, 4096], BF16)
    rot = ctx.sbuf("l0a_rot", [128, 512], F32)
    tmt = ctx.sbuf("l0a_tmt", [128, 1024], BF16)
    rtmp = ctx.sbuf("l0a_rtmp", [128, 512], F32)
    junk = ctx.sbuf("l0a_junk", [128, 512], BF16)
    slab_used = None
    rot_used = None
    tm_used = [None, None]
    xlt_used = [None, None]
    tr_toks = []
    vt_toks = []
    for j, half in [(0, 0), (0, 1), (1, 0), (1, 1), (2, 0), (2, 1)]:
        col0 = j * 1024 + half * 512
        t_sl = ctx.dma("gpsimd", slab[:], qkv3[:, :, col0:col0 + 512], deps=[slab_used])
        for tt in range(16):
            par_x = tt % 2
            t_x = ctx.dma("sync", xlt[:, par_x * 1024:(par_x + 1) * 1024],
                          xlp[:, :, tt * 128:(tt + 1) * 128], deps=[xlt_used[par_x]])
            ps = ctx.psum[tt % 2]
            pairs = [(xlt[:, par_x * 1024 + kc * 128: par_x * 1024 + kc * 128 + 128],
                      slab[:, kc * 512:(kc + 1) * 512], [t_sl, t_x]) for kc in range(NCH)]
            mm_group(ctx, ps, ps.ap[:, :], pairs)
            xlt_used[par_x] = ctx.last["tensor"]
            if j == 2:
                tv = evac(ctx, "vector", ps, lambda e, tt=tt, half=half, ps=ps: e.tensor_scalar(
                    vtm[:, tt * 1024 + half * 512: tt * 1024 + half * 512 + 512],
                    ps.ap[:, :], lamc, None, ALU.mult, ALU.bypass), deps=[t_lamc])
                slab_used = tv
                vt_toks.append(tv)
                continue
            pidx = tt % 2
            tile = tmt[:, pidx * 512:(pidx + 1) * 512]
            t0 = evac(ctx, "vector", ps, lambda e, tile=tile, ps=ps: e.tensor_copy(
                tile, ps.ap[:, :]), deps=[tm_used[pidx]])
            slab_used = t0
            hts = []
            prev_rms = t0
            for h4 in range(4):
                hsl = slice(h4 * 128, h4 * 128 + 128)
                t1 = ctx.op("vector", lambda e, tile=tile, hsl=hsl: e.scalar_tensor_tensor(
                    junk[:, 0:128], tile[:, hsl], 1.0, tile[:, hsl], ALU.mult, ALU.mult,
                    accum_out=sm[:, 0:1]), deps=[prev_rms], sig=True)
                t2 = ctx.op("vector", lambda e: e.tensor_scalar(
                    sm[:, 1:2], sm[:, 0:1], 1.0 / 128.0, 1e-6,
                    ALU.mult, ALU.add), deps=[t1], sig=True)
                t3 = ctx.op("scalar", lambda e: e.activation(
                    sm[:, 2:3], sm[:, 1:2], AF.Sqrt, scale=1.0), deps=[t2], sig=True)
                t4 = ctx.op("vector", lambda e: e.reciprocal(
                    sm[:, 3:4], sm[:, 2:3]), deps=[t3], sig=True)
                t5 = ctx.op("vector", lambda e, tile=tile, hsl=hsl: e.tensor_scalar(
                    tile[:, hsl], tile[:, hsl], sm[:, 3:4], None,
                    ALU.mult, ALU.bypass), deps=[t4], sig=True)
                hts.append(t5)
                prev_rms = t5
            t_rot = ctx.dma("sync", rot[:], W["L0a_rot"][tt * 128:(tt + 1) * 128, :],
                            deps=[rot_used])
            v3 = tile.rearrange("p (h d) -> p h d", h=4)
            lo = v3[:, :, 0:64]
            hi = v3[:, :, 64:128]
            cos3 = rot[:, 0:256].rearrange("p (h f) -> p h f", h=4)
            sin3 = rot[:, 256:512].rearrange("p (h f) -> p h f", h=4)
            la = rtmp[:, 0:256].rearrange("p (h f) -> p h f", h=4)
            lb = rtmp[:, 256:512].rearrange("p (h f) -> p h f", h=4)
            d0 = hts + [t_rot]
            b1 = ctx.op("vector", lambda e, lo=lo, la=la, cos3=cos3: e.tensor_tensor(la, lo, cos3, op=ALU.mult), deps=d0, sig=True)
            b2 = ctx.op("vector", lambda e, hi=hi, lb=lb, sin3=sin3: e.tensor_tensor(lb, hi, sin3, op=ALU.mult), deps=d0, sig=True)
            b3 = ctx.op("vector", lambda e, lo=lo, sin3=sin3: e.tensor_tensor(lo, lo, sin3, op=ALU.mult), deps=[b1], sig=True)
            b4 = ctx.op("vector", lambda e, hi=hi, cos3=cos3: e.tensor_tensor(hi, hi, cos3, op=ALU.mult), deps=[b2], sig=True)
            b5 = ctx.op("vector", lambda e, la=la, lb=lb: e.tensor_tensor(la, la, lb, op=ALU.add), deps=[b3, b4], sig=True)
            b6 = ctx.op("vector", lambda e, lo=lo, hi=hi: e.tensor_tensor(hi, hi, lo, op=ALU.subtract), deps=[b5], sig=True)
            b7 = ctx.op("vector", lambda e, lo=lo, la=la: e.tensor_copy(lo, la), deps=[b6], sig=True)
            rot_used = b4
            dst = qfm if j == 0 else kfm
            last = None
            for h4 in range(4):
                hg = half * 4 + h4
                psb = ctx.psum[6 + h4 % 2]
                transpose128(ctx, psb, psb.ap[0:128, 0:128],
                             tile[:, h4 * 128:(h4 + 1) * 128], c["c_ident_bf"][:],
                             deps=[b6, b7])
                tr = evac(ctx, "vector", psb, lambda e, dst=dst, hg=hg, tt=tt, psb=psb: e.tensor_copy(
                    dst[:, hg * T + tt * 128: hg * T + tt * 128 + 128],
                    psb.ap[0:128, 0:128]))
                tr_toks.append(tr)
                last = tr
            tm_used[pidx] = last
    ctx.barrier_all()
    ctx.sbfree(junk, rtmp, tmt, rot, slab, xlt)
    if dbg in ("qfm", "kfm", "vtm"):
        srcb = {"qfm": qfm, "kfm": kfm, "vtm": vtm}[dbg]
        ctx.op("vector", lambda e, srcb=srcb: e.tensor_copy(attn[:], srcb[:]), sig=True)
        ctx.barrier_all()
        ctx.sbfree(vtm, kfm, qfm)
        yb = ctx.sbuf("l0a_dbg", [128, NCH * T], F32)
        ctx.op("vector", lambda e: e.tensor_copy(yb[:], attn[:]), sig=True)
        ctx.barrier_all()
        return yb

    ckfm = ctx.sbuf("l0a_ckfm", [128, 1024], BF16)
    cvfm = ctx.sbuf("l0a_cvfm", [128, 1024], BF16)
    kpc = ctx.sbuf("l0a_kpc", [128, 256], F32)
    memc = ctx.sbuf("l0a_memc", [128, 8], BF16)
    t_kp = ctx.dma("sync", kpc[:, 0:128], W["L0a_kpos"])
    t_vp = ctx.dma("sync", kpc[:, 128:256], W["L0a_vpos"])
    t_mk = ctx.dma("gpsimd", memc[:], W["L0a_memk"])
    arr = ctx.sbuf("l0a_arr", [128, 16384], BF16)
    fcsl = [ctx.sbuf("l0a_fcsl0", [128, 2048], BF16)]
    pjsl = ctx.sbuf("l0a_pjsl", [128, 1024], BF16)
    hmt = ctx.sbuf("l0a_hmt", [128, 1024], BF16)
    cm_toks = []
    for side in range(2):
        fcw3 = (W["L0a_kfc"] if side == 0 else W["L0a_vfc"]).rearrange(
            "(kc p) n -> p kc n", p=128)
        pjw3 = (W["L0a_kpj"] if side == 0 else W["L0a_vpj"]).rearrange(
            "(mc p) n -> p mc n", p=128)
        cdst = ckfm if side == 0 else cvfm
        arr_toks = []
        if side == 0:
            for h in range(8):
                kv = kfm[:, h * T:(h + 1) * T].rearrange("p (w s) -> p w s", s=16)
                for s in range(16):
                    t = ctx.op("vector" if (h + s) % 2 == 0 else "gpsimd",
                        lambda e, h=h, s=s, kv=kv: e.tensor_scalar(
                            arr[:, s * 1024 + h * 128: s * 1024 + h * 128 + 128],
                            kv[:, :, s],
                            kpc[:, h * 16 + s: h * 16 + s + 1], None, ALU.add, ALU.bypass),
                        deps=[t_kp] + tr_toks, sig=True)
                    arr_toks.append(t)
        else:
            av_ = arr[:].rearrange("p (s x) -> p x s", s=16)
            sc_t = []
            for h in range(8):
                for tt in range(16):
                    vin = vtm[:, tt * 1024 + (h // 4) * 512 + (h % 4) * 128:
                              tt * 1024 + (h // 4) * 512 + (h % 4) * 128 + 128]
                    psb = ctx.psum[6 + tt % 2]
                    transpose128(ctx, psb, psb.ap[0:128, 0:128], vin,
                                 c["c_ident_bf"][:], deps=vt_toks + cm_toks)
                    pv = psb.ap[0:128, 0:128].rearrange("p (w s) -> p w s", s=16)
                    ov = av_[:, h * 128 + tt * 8: h * 128 + tt * 8 + 8, :]
                    sc_t.append(evac(ctx, "vector", psb,
                        lambda e, ov=ov, pv=pv: e.tensor_copy(ov, pv)))
            for h in range(8):
                for s in range(16):
                    t = ctx.op("vector" if (h + s) % 2 == 0 else "gpsimd",
                        lambda e, h=h, s=s: e.tensor_scalar(
                            arr[:, s * 1024 + h * 128: s * 1024 + h * 128 + 128],
                            arr[:, s * 1024 + h * 128: s * 1024 + h * 128 + 128],
                            kpc[:, 128 + h * 16 + s: 128 + h * 16 + s + 1], None,
                            ALU.add, ALU.bypass),
                        deps=[t_vp] + sc_t, sig=True)
                    arr_toks.append(t)
        if dbg == "arr" and side == 0:
            ctx.op("vector", lambda e: e.tensor_copy(attn[:], arr[:]), deps=arr_toks, sig=True)
            ctx.barrier_all()
            ctx.sbfree(hmt, pjsl, fcsl[0], arr, memc, kpc, cvfm, ckfm, vtm, kfm, qfm)
            yb = ctx.sbuf("l0a_dbg", [128, NCH * T], F32)
            ctx.op("vector", lambda e: e.tensor_copy(yb[:], attn[:]), sig=True)
            ctx.barrier_all()
            return yb
        psC = ctx.psum[2]
        psD = ctx.psum[3]
        fc_used = [None, None]
        pj_used = None
        t_pj = None
        hm_used = [None, None]
        for m in range(64):
            par = 0
            if m % 8 == 0:
                t_pj = ctx.dma("gpsimd", pjsl[:],
                               pjw3[:, (m // 8) * 8:(m // 8) * 8 + 8, :], deps=[pj_used])
            t_fc = ctx.dma("gpsimd", fcsl[par][:], fcw3[:, :, m * 128:(m + 1) * 128],
                           deps=[fc_used[par]])
            psA = ctx.psum[m % 2]
            hm = hmt[:, 0:1024]
            lastq = None
            for nh in range(2):
                pairs = [(fcsl[par][:, s * 128:(s + 1) * 128],
                          arr[:, s * 1024 + nh * 512: s * 1024 + nh * 512 + 512],
                          [t_fc] + arr_toks) for s in range(16)]
                mm_group(ctx, psA, psA.ap[:, :], pairs)
                t_r = evac(ctx, "scalar", psA, lambda e, hm=hm, nh=nh, psA=psA: e.activation(
                    hm[:, nh * 512:(nh + 1) * 512], psA.ap[:, :], AF.Relu, scale=1.0),
                    deps=[hm_used[par]])
                t_q = ctx.op("vector", lambda e, hm=hm, nh=nh: e.scalar_tensor_tensor(
                    hm[:, nh * 512:(nh + 1) * 512], hm[:, nh * 512:(nh + 1) * 512], 1.0,
                    hm[:, nh * 512:(nh + 1) * 512], ALU.mult, ALU.mult), deps=[t_r], sig=True)
                psE = psC if nh == 0 else psD
                t_p = mm(ctx, psE, psE.ap[:, :], pjsl[:, (m % 8) * 128:(m % 8) * 128 + 128],
                         hm[:, nh * 512:(nh + 1) * 512], start=(m == 0), stop=(m == 63),
                         deps=[t_q, t_pj], sig=True)
                hm_used[par] = t_p
                lastq = t_p
            fc_used[par] = lastq
            if m % 8 == 7:
                pj_used = hm_used[par]
            if dbg == "fc0" and side == 0 and m == 0:
                ctx.op("vector", lambda e: e.tensor_copy(attn[:, 0:1024], hm[:]), sig=True)
                ctx.barrier_all()
                ctx.sbfree(hmt, pjsl, fcsl[0], arr, memc, kpc, cvfm, ckfm, vtm, kfm, qfm)
                yb = ctx.sbuf("l0a_dbg", [128, NCH * T], F32)
                ctx.op("vector", lambda e: e.tensor_copy(yb[:], attn[:]), sig=True)
                ctx.barrier_all()
                return yb
        cm_toks = [
            evac(ctx, "vector", psC, lambda e, cdst=cdst, psC=psC: e.tensor_copy(
                cdst[:, 0:512], psC.ap[:, :])),
            evac(ctx, "vector", psD, lambda e, cdst=cdst, psD=psD: e.tensor_copy(
                cdst[:, 512:1024], psD.ap[:, :])),
        ]
    ctx.barrier_all()
    ctx.sbfree(hmt, pjsl, fcsl[0], arr)
    imp = ctx.sbuf("l0a_imp", [128, T], F32)
    imp_toks = []
    for n in range(NT):
        imp_toks.append(ctx.op("gpsimd", lambda e, n=n: e.memset(
            imp[:, n * 512:(n + 1) * 512], 0.0), sig=True))
    memvr = ctx.sbuf("l0a_memvr", [1, 1024], BF16)
    t_mv = ctx.dma("gpsimd", memvr[:], W["L0a_memv"])
    if dbg == "ck":
        ctx.op("vector", lambda e: e.tensor_copy(attn[:, 0:1024], ckfm[:]), sig=True)
        ctx.op("vector", lambda e: e.tensor_copy(attn[:, 2048:3072], cvfm[:]), sig=True)
        ctx.barrier_all()
        ctx.sbfree(memvr, imp, memc, kpc, cvfm, ckfm, vtm, kfm, qfm)
        yb = ctx.sbuf("l0a_dbg", [128, NCH * T], F32)
        ctx.op("vector", lambda e: e.tensor_copy(yb[:], attn[:]), sig=True)
        ctx.barrier_all()
        return yb

    cvtm = ctx.sbuf("l0a_cvtm", [128, 1024], BF16)
    cvt_toks = []
    for h in range(8):
        psb = ctx.psum[6 + h % 2]
        transpose128(ctx, psb, psb.ap[0:128, 0:128], cvfm[:, h * 128:(h + 1) * 128],
                     c["c_ident_bf"][:], deps=cm_toks)
        cvt_toks.append(evac(ctx, "vector", psb, lambda e, h=h, psb=psb: e.tensor_copy(
            cvtm[:, h * 128:(h + 1) * 128], psb.ap[0:128, 0:128])))
    cmc, rampt, msk_toks = _l0a_masks(ctx, c, "l0a")
    ptile = ctx.sbuf("l0a_ptc", [128, 512], BF16)
    stile = ctx.sbuf("l0a_stc", [128, 512], F32)
    rowsc = ctx.sbuf("l0a_rwc", [1, 1024], F32)
    mrow = ctx.sbuf("l0a_mrow", [1, 512], BF16)
    crow = rowsc[0:1, 0:512]
    grow = rowsc[0:1, 512:1024]

    imp_r = list(imp_toks)
    for h in range(8):
        for qt in range(4):
            psS = ctx.psum[qt % 2]
            q_rhs = qfm[:, h * T + qt * 512: h * T + qt * 512 + 512]
            mm(ctx, psS, psS.ap[:, :], ckfm[:, h * 128:(h + 1) * 128], q_rhs,
               start=True, stop=True, deps=cm_toks + tr_toks)
            psM = ctx.psum[4]
            mm(ctx, psM, psM.ap[0:1, :], memc[:, h:h + 1], q_rhs, start=True, stop=True,
               deps=[t_mk])
            t_imp = evac(ctx, "vector", psS, lambda e, qt=qt, psS=psS: e.tensor_tensor(
                imp[:, qt * 512:(qt + 1) * 512], imp[:, qt * 512:(qt + 1) * 512],
                psS.ap[:, :], op=ALU.add), deps=[imp_r[qt]])
            imp_r[qt] = t_imp
            t_e = ctx.op("scalar", lambda e, psS=psS: e.activation(
                ptile[:], psS.ap[:, :], AF.Exp, scale=0.12, bias=c["c_cols"][:, 5:6]),
                deps=[t_imp], sig=True)
            psS.tok_r = t_e
            t_msk = ctx.op("vector", lambda e, qt=qt: e.tensor_tensor(
                ptile[:], ptile[:], cmc[:, qt * 512:(qt + 1) * 512], op=ALU.mult),
                deps=[t_e] + msk_toks, sig=True)
            t_me = evac(ctx, "scalar", psM, lambda e, psM=psM: e.activation(
                mrow[:], psM.ap[0:1, :], AF.Exp, scale=0.12, bias=c["c_cols"][0:1, 5:6]))
            psSum = ctx.psum[5]
            mm(ctx, psSum, psSum.ap[0:1, :], c["c_onescol_bf"][:], ptile[:],
               start=True, stop=False, deps=[t_msk])
            mm(ctx, psSum, psSum.ap[0:1, :], c["c_onescol_bf"][0:1, 0:1], mrow[:],
               start=False, stop=True, deps=[t_me])
            psAV = ctx.psum[2]
            mm(ctx, psAV, psAV.ap[:, :], cvtm[:, h * 128:(h + 1) * 128], ptile[:],
               start=True, stop=False, deps=[t_msk] + cvt_toks)
            mm(ctx, psAV, psAV.ap[:, :], memvr[0:1, h * 128:(h + 1) * 128], mrow[:],
               start=False, stop=True, deps=[t_me, t_mv])
            t_rc = evac(ctx, "vector", psSum, lambda e, psSum=psSum: e.reciprocal(
                crow, psSum.ap[0:1, :]))
            psG = ctx.psum[3]
            mm(ctx, psG, psG.ap[0:1, :], c["c_ident_bf"][0:24, 3 * h: 3 * h + 1],
               gfm[0:24, qt * 512:(qt + 1) * 512], start=True, stop=True, deps=gf_toks)
            t_g = evac(ctx, "vector", psG, lambda e, psG=psG: e.tensor_copy(
                grow, psG.ap[0:1, :]))
            t_gg = ctx.op("vector", lambda e: e.tensor_tensor(
                grow, grow, crow, op=ALU.mult), deps=[t_g, t_rc], sig=True)
            psB = ctx.psum[1 - qt % 2]
            mm(ctx, psB, psB.ap[:, :], c["c_onesrow"][:], grow, start=True, stop=True,
               deps=[t_gg])
            t_st = evac(ctx, "vector", psAV, lambda e, psAV=psAV: e.tensor_copy(
                stile[:], psAV.ap[:, :]))
            ptile_done = ctx.last["tensor"]
            evac(ctx, "vector", psB, lambda e, h=h, qt=qt, psB=psB: e.tensor_tensor(
                attn[:, h * T + qt * 512: h * T + qt * 512 + 512], stile[:],
                psB.ap[:, :], op=ALU.mult), deps=[t_st])
    ctx.barrier_all()
    ctx.sbfree(mrow, rowsc, stile, ptile)
    if dbg == "comp":
        ctx.sbfree(rampt, cmc, cvtm, memvr, imp, memc, kpc, cvfm, ckfm, vtm, kfm, qfm)
        yb = ctx.sbuf("l0a_dbg", [128, NCH * T], F32)
        ctx.op("vector", lambda e: e.tensor_copy(yb[:], attn[:]), sig=True)
        ctx.barrier_all()
        return yb

    selT = ctx.sbuf("l0a_selT", [128, T], BF16)
    sct = ctx.sbuf("l0a_sct", [128, 512], F32)
    stm = ctx.sbuf("l0a_stm", [128, 384], F32)
    sel_done = []
    for qt in range(4):
        t_s1 = ctx.op("vector", lambda e, qt=qt: e.tensor_tensor(
            sct[:], imp[:, qt * 512:(qt + 1) * 512], rampt[:], op=ALU.subtract),
            deps=[imp_r[qt]] + msk_toks + sel_done, sig=True)
        t_s2 = ctx.op("vector", lambda e, qt=qt: e.tensor_tensor(
            sct[:], sct[:], cmc[:, qt * 512:(qt + 1) * 512], op=ALU.mult),
            deps=[t_s1], sig=True)
        t_sc = ctx.op("vector", lambda e: e.tensor_tensor(
            sct[:], sct[:], rampt[:], op=ALU.add), deps=[t_s2], sig=True)
        for sub in range(4):
            ps = ctx.psum[4]
            transpose128(ctx, ps, ps.ap[0:128, 0:128], sct[:, sub * 128:(sub + 1) * 128],
                         c["c_ident"][:], deps=[t_sc])
            t_w = evac(ctx, "vector", ps, lambda e, ps=ps: e.tensor_copy(
                stm[:, 0:128], ps.ap[0:128, 0:128]))
            t_o = ctx.op("vector", lambda e: e.tensor_copy(stm[:, 128:256], stm[:, 0:128]),
                         deps=[t_w], sig=True)
            prev = t_o
            for it in range(16):
                t_m = ctx.op("vector", lambda e: e.reduce_max(
                    sm[:, 4:5], stm[:, 0:128], axis=AX.X), deps=[prev], sig=True)
                if it < 15:
                    t_eq = ctx.op("vector", lambda e: e.tensor_scalar(
                        stm[:, 256:384], stm[:, 0:128], sm[:, 4:5], -1e30,
                        ALU.is_equal, ALU.mult), deps=[t_m], sig=True)
                    prev = ctx.op("vector", lambda e: e.tensor_tensor(
                        stm[:, 0:128], stm[:, 0:128], stm[:, 256:384], op=ALU.add),
                        deps=[t_eq], sig=True)
                else:
                    prev = t_m
            t_sel = ctx.op("vector", lambda e: e.tensor_scalar(
                stm[:, 0:128], stm[:, 128:256], sm[:, 4:5], None, ALU.is_ge, ALU.bypass),
                deps=[prev], sig=True)
            ps2 = ctx.psum[5]
            transpose128(ctx, ps2, ps2.ap[0:128, 0:128], stm[:, 0:128],
                         c["c_ident"][:], deps=[t_sel])
            sel_done.append(evac(ctx, "vector", ps2,
                lambda e, qt=qt, sub=sub, ps2=ps2: e.tensor_copy(
                    selT[:, qt * 512 + sub * 128: qt * 512 + sub * 128 + 128],
                    ps2.ap[0:128, 0:128])))
    ctx.barrier_all()
    ctx.sbfree(stm, sct)
    if dbg == "sel":
        ctx.op("vector", lambda e: e.tensor_copy(attn[:, 0:T], selT[:]), sig=True)
        ctx.barrier_all()
        ctx.sbfree(selT, rampt, cmc, cvtm, memvr, imp, memc, kpc, cvfm, ckfm, vtm, kfm, qfm)
        yb = ctx.sbuf("l0a_dbg", [128, NCH * T], F32)
        ctx.op("vector", lambda e: e.tensor_copy(yb[:], attn[:]), sig=True)
        ctx.barrier_all()
        return yb

    etm = ctx.sbuf("l0a_etm", [128, 2048], BF16)
    sbm = ctx.sbuf("l0a_sbm", [128, 2048], BF16)
    slm = ctx.sbuf("l0a_slm", [128, 2560], BF16)
    t_et = ctx.dma("sync", etm[:], W["L0a_et"])
    t_sbm = ctx.dma("sync", sbm[:], W["L0a_sbm"])
    t_slm = ctx.dma("sync", slm[:], W["L0a_slm"])
    ptile2 = ctx.sbuf("l0a_pt2", [128, 1024], BF16)
    mtmp = ctx.sbuf("l0a_mt2", [128, 512], BF16)
    stile2 = ctx.sbuf("l0a_st2", [128, 512], F32)
    crow2 = c["rows"][0:1, 0:512]
    grow2 = c["rows"][0:1, 512:1024]
    for h in range(8):
        for qt in range(4):
            q_rhs = qfm[:, h * T + qt * 512: h * T + qt * 512 + 512]
            vcol = lambda kt, h=h: vtm[:, kt * 1024 + (h // 4) * 512 + (h % 4) * 128:
                                       kt * 1024 + (h // 4) * 512 + (h % 4) * 128 + 128]
            psAV = ctx.psum[2]
            psSum = ctx.psum[5]
            nkt = 4 * qt + 4
            for kt in range(nkt):
                psS = ctx.psum[kt % 2]
                mm(ctx, psS, psS.ap[:, :], kfm[:, h * T + kt * 128: h * T + kt * 128 + 128],
                   q_rhs, start=True, stop=True, deps=tr_toks if kt < 2 else ())
                pe = ptile2[:, (kt % 2) * 512:(kt % 2) * 512 + 512]
                t_e = evac(ctx, "scalar", psS, lambda e, pe=pe, psS=psS: e.activation(
                    pe, psS.ap[:, :], AF.Exp, scale=0.12, bias=c["c_cols"][:, 5:6]))
                psE = ctx.psum[4]
                mm(ctx, psE, psE.ap[:, :], etm[:, kt * 128:(kt + 1) * 128],
                   selT[:, qt * 512:(qt + 1) * 512], start=True, stop=True,
                   deps=[t_et] + sel_done)
                if kt >= 4 * qt:
                    jj = kt - 4 * qt
                    t_m1 = evac(ctx, "vector", psE, lambda e, jj=jj, psE=psE: e.tensor_tensor(
                        mtmp[:], psE.ap[:, :], sbm[:, jj * 512:(jj + 1) * 512],
                        op=ALU.max), deps=[t_sbm])
                    t_m2 = ctx.op("vector", lambda e, jj=jj: e.tensor_tensor(
                        mtmp[:], mtmp[:], c["cmask"][:, jj * 512:(jj + 1) * 512],
                        op=ALU.mult), deps=[t_m1, c["cmask_tok"]], sig=True)
                    t_f = ctx.op("vector", lambda e, pe=pe: e.tensor_tensor(
                        pe, pe, mtmp[:], op=ALU.mult), deps=[t_m2, t_e], sig=True)
                else:
                    t_f = evac(ctx, "vector", psE, lambda e, pe=pe, psE=psE: e.tensor_tensor(
                        pe, pe, psE.ap[:, :], op=ALU.mult), deps=[t_e])
                psS.tok_r = t_f
                mm(ctx, psAV, psAV.ap[:, :], vcol(kt), pe,
                   start=(kt == 0), stop=(kt == nkt - 1),
                   deps=[t_f] + (vt_toks if kt < 2 else []))
                mm(ctx, psSum, psSum.ap[0:1, :], c["c_onescol_bf"][:], pe,
                   start=(kt == 0), stop=(kt == nkt - 1), deps=[t_f])
            t_rc = evac(ctx, "vector", psSum, lambda e, psSum=psSum: e.reciprocal(
                crow2, psSum.ap[0:1, :]))
            psG = ctx.psum[3]
            mm(ctx, psG, psG.ap[0:1, :], c["c_ident_bf"][0:24, 3 * h + 1: 3 * h + 2],
               gfm[0:24, qt * 512:(qt + 1) * 512], start=True, stop=True, deps=gf_toks)
            t_g = evac(ctx, "vector", psG, lambda e, psG=psG: e.tensor_copy(
                grow2, psG.ap[0:1, :]))
            t_gg = ctx.op("vector", lambda e: e.tensor_tensor(
                grow2, grow2, crow2, op=ALU.mult), deps=[t_g, t_rc], sig=True)
            psB = ctx.psum[4]
            mm(ctx, psB, psB.ap[:, :], c["c_onesrow"][:], grow2, start=True, stop=True,
               deps=[t_gg])
            t_st = evac(ctx, "vector", psAV, lambda e, psAV=psAV: e.tensor_copy(
                stile2[:], psAV.ap[:, :]))
            t_sb2 = evac(ctx, "vector", psB, lambda e, psB=psB: e.tensor_tensor(
                stile2[:], stile2[:], psB.ap[:, :], op=ALU.mult), deps=[t_st])
            t_acc = ctx.op("vector", lambda e, h=h, qt=qt: e.tensor_tensor(
                attn[:, h * T + qt * 512: h * T + qt * 512 + 512],
                attn[:, h * T + qt * 512: h * T + qt * 512 + 512],
                stile2[:], op=ALU.add), deps=[t_sb2], sig=True)
            psAV2 = ctx.psum[2]
            psSum2 = ctx.psum[5]
            kt0 = max(0, 4 * qt - 1)
            kts = list(range(kt0, 4 * qt + 4))
            for ki, kt in enumerate(kts):
                psS = ctx.psum[kt % 2]
                mm(ctx, psS, psS.ap[:, :], kfm[:, h * T + kt * 128: h * T + kt * 128 + 128],
                   q_rhs, start=True, stop=True, deps=())
                pe = ptile2[:, (kt % 2) * 512:(kt % 2) * 512 + 512]
                jj = kt - 4 * qt
                t_e = evac(ctx, "scalar", psS, lambda e, pe=pe, psS=psS: e.activation(
                    pe, psS.ap[:, :], AF.Exp, scale=0.12, bias=c["c_cols"][:, 5:6]))
                t_f = ctx.op("vector", lambda e, pe=pe, jj=jj: e.tensor_tensor(
                    pe, pe, slm[:, (jj + 1) * 512:(jj + 2) * 512], op=ALU.mult),
                    deps=[t_e, t_slm], sig=True)
                psS.tok_r = t_f
                mm(ctx, psAV2, psAV2.ap[:, :], vcol(kt), pe,
                   start=(ki == 0), stop=(ki == len(kts) - 1), deps=[t_f])
                mm(ctx, psSum2, psSum2.ap[0:1, :], c["c_onescol_bf"][:], pe,
                   start=(ki == 0), stop=(ki == len(kts) - 1), deps=[t_f])
            t_rc = evac(ctx, "vector", psSum2, lambda e, psSum2=psSum2: e.reciprocal(
                crow2, psSum2.ap[0:1, :]))
            psG = ctx.psum[3]
            mm(ctx, psG, psG.ap[0:1, :], c["c_ident_bf"][0:24, 3 * h + 2: 3 * h + 3],
               gfm[0:24, qt * 512:(qt + 1) * 512], start=True, stop=True, deps=gf_toks)
            t_g = evac(ctx, "vector", psG, lambda e, psG=psG: e.tensor_copy(
                grow2, psG.ap[0:1, :]))
            t_gg = ctx.op("vector", lambda e: e.tensor_tensor(
                grow2, grow2, crow2, op=ALU.mult), deps=[t_g, t_rc], sig=True)
            psB = ctx.psum[4]
            mm(ctx, psB, psB.ap[:, :], c["c_onesrow"][:], grow2, start=True, stop=True,
               deps=[t_gg])
            t_st = evac(ctx, "vector", psAV2, lambda e, psAV2=psAV2: e.tensor_copy(
                stile2[:], psAV2.ap[:, :]), deps=[t_acc])
            t_sb2 = evac(ctx, "vector", psB, lambda e, psB=psB: e.tensor_tensor(
                stile2[:], stile2[:], psB.ap[:, :], op=ALU.mult), deps=[t_st])
            ctx.op("vector", lambda e, h=h, qt=qt: e.tensor_tensor(
                attn[:, h * T + qt * 512: h * T + qt * 512 + 512],
                attn[:, h * T + qt * 512: h * T + qt * 512 + 512],
                stile2[:], op=ALU.add), deps=[t_sb2], sig=True)
    ctx.barrier_all()
    ctx.sbfree(stile2, mtmp, ptile2, slm, sbm, etm)
    ctx.sbfree(selT)
    ctx.sbfree(rampt, cmc)
    ctx.sbfree(cvtm)
    ctx.sbfree(memvr, imp, memc, kpc, cvfm, ckfm)
    if dbg == "attn":
        ctx.sbfree(vtm, kfm, qfm)
        yb = ctx.sbuf("l0a_dbg", [128, NCH * T], F32)
        ctx.op("vector", lambda e: e.tensor_copy(yb[:], attn[:]), sig=True)
        ctx.barrier_all()
        return yb

    ctx.sbfree(vtm, kfm, qfm)
    yb = ctx.sbuf("l0a_yb", [128, NCH * T], BF16)
    ych = lambda ch: yb[:, ch * T:(ch + 1) * T]
    cp3 = W["L0a_cproj"].rearrange("(kc p) n -> p kc n", p=128)
    wslab = ctx.sbuf("l0a_wsl", [128, 1024], BF16)
    wsl_used = None
    for mch in range(NCH):
        wt = ctx.dma("gpsimd", wslab[:], cp3[:, :, mch * 128:(mch + 1) * 128],
                     deps=[wsl_used])
        for n in range(NT):
            ps = ctx.psum[n % 2]
            pairs = [(wslab[:, kc * 128:(kc + 1) * 128],
                      attn[:, kc * T + n * 512: kc * T + n * 512 + 512], [wt])
                     for kc in range(NCH)]
            mm_group(ctx, ps, ps.ap[:, :], pairs)
            wsl_used = evac(ctx, "vector", ps, lambda e, mch=mch, n=n, ps=ps: e.tensor_copy(
                ych(mch)[:, n * 512:(n + 1) * 512], ps.ap[:, :]))
    ctx.barrier_all()
    ctx.sbfree(wslab)
    xres2 = ctx.sbuf("xres", [128, NCH * T], F32, side="right")
    rtoks = [ctx.dma("sync", xres2[:, ch * T:(ch + 1) * T],
                     park[ch * 128:(ch + 1) * 128, :]) for ch in range(NCH)]
    x2ch = lambda ch: xres2[:, ch * T:(ch + 1) * T]
    ln_to(ctx, c, ych, x2ch, accumulate=True, acc_dst=x2ch, deps=rtoks, src_f32=False,
          sq_buf=lnsq_b, tmp_buf=lntmp_b)
    ctx.barrier_all()
    ctx.sbfree(yb, attn, lam, sbc, sw, gfm, lntmp_b, lnsq_b)
    return xres2


WEIGHT_SPECS["l0a"] = {
    "L0a_qkv": [1024, 3072], "L0a_cproj": [1024, 1024],
    "L0a_kfc": [2048, 8192], "L0a_kpj": [8192, 128],
    "L0a_vfc": [2048, 8192], "L0a_vpj": [8192, 128],
    "L0a_strat": [1024, 24], "L0a_stratb": [24, 1],
    "L0a_kpos": [128, 128], "L0a_vpos": [128, 128],
    "L0a_memk": [128, 8], "L0a_memv": [1, 1024],
    "L0a_lam": [1, 2], "L0a_rot": [2048, 512],
    "L0a_et": ([128, 2048], "bf16"), "L0a_sbm": ([128, 2048], "bf16"),
    "L0a_slm": ([128, 2560], "bf16"),
}
BLOCK_EMIT["l0a"] = block_l0_attn



ALL_BLOCKS = ["l0a", "l0f", "l1a", "l1f"]


def kernel(**inputs):
    x_full = np.ascontiguousarray(np.asarray(inputs["L0_attn_x"], np.float32))
    out = run_blocks(x_full, inputs, ALL_BLOCKS, n_cores=8)
    return np.ascontiguousarray(out).astype(np.float32)
